# version 03 (final)
# baseline (speedup 1.0000x reference)
import numpy as np
import ml_dtypes

import concourse.bass as bass
import concourse.mybir as mybir
from concourse.bass_utils import run_bass_kernel_spmd

F32 = mybir.dt.float32
BF16 = mybir.dt.bfloat16
bf16 = ml_dtypes.bfloat16
AX = mybir.AxisListType.X
AF = mybir.ActivationFunctionType
ALU = mybir.AluOpType

S = 4096
D = 1024
NCH = 8
JW = 512
EPS_CLIP = 1e-5
QB = 128.0
EXP_SHIFT = -4.0


def build_nc(maxphase=99, dsub=99, nst=99, do_epi=True, dbg=False):
    nc = bass.Bass("TRN2", target_bir_lowering=False, detect_race_conditions=False)

    dd = {}
    for name, shape, dt in [
        ("x_own", [2048, D], F32), ("x_nat", [S, D], F32), ("xT", [D, S], F32),
        ("xTo", [D, 2048], F32), ("keysT", [D, S], F32), ("valuesT", [D, S], F32),
        ("wqT", [D, JW], F32), ("wkT", [D, JW], F32), ("wvT", [D, 520], F32),
        ("bq", [128, 4], F32), ("bk", [128, 4], F32), ("bv", [1, 520], F32),
        ("wf", [D, D], F32), ("gln", [128, 8], F32), ("bln", [128, 8], F32),
        ("eye32", [128, 128], F32), ("ones1bf", [1, 128], BF16),
        ("ones1f", [1, 128], F32), ("em4", [128, 1], F32),
        ("sel16", [16, 1024], F32),
    ]:
        dd[name] = nc.dram_tensor(name, shape, dt, kind="ExternalInput").ap()
    d_out = nc.dram_tensor("out", [2048, D], F32, kind="ExternalOutput").ap()
    if dbg:
        dbg_outs = {n: nc.dram_tensor("dbg_" + n, shp, dt2, kind="ExternalOutput").ap()
                    for n, shp, dt2 in [("qT", [128, 16384], BF16), ("kT", [128, 16384], BF16),
                                        ("v1", [128, 16640], BF16), ("wq", [128, 8192], BF16),
                                        ("pool", [128, 32768], BF16),
                                        ("wfreg", [128, 6144], F32),
                                        ("stmu", [128, 32], F32),
                                        ("str", [128, 32], F32),
                                        ("stsums", [128, 32], F32),
                                        ("stsums2", [128, 32], F32),
                                        ("beta", [128, 1], F32), ("thr", [128, 1], F32),
                                        ("nthr", [128, 1], F32), ("b11", [1, 1], F32)]}

    qT_all = nc.alloc_sbuf_tensor("qT_all", [128, 4 * S], BF16)
    kT_all = nc.alloc_sbuf_tensor("kT_all", [128, 4 * S], BF16)
    v1_all = nc.alloc_sbuf_tensor("v1_all", [128, 32 * 520], BF16)
    w_q = nc.alloc_sbuf_tensor("w_q", [128, 8 * D], BF16)
    pool = nc.alloc_sbuf_tensor("pool", [128, 32768], BF16)
    wfreg = nc.alloc_sbuf_tensor("wfreg", [128, 6144], F32)

    poolf = pool.bitcast(F32)
    m3f = v1_all.bitcast(F32)
    m4f = w_q.bitcast(F32)
    wfbf = wfreg.bitcast(BF16)

    wq_bf = pool[:, 0:4096]
    aT_pair = poolf[:, 9216:13312]
    xn2 = lambda kc: pool[:, 2048 * kc:2048 * (kc + 1)]
    mu_rep = poolf[:, 2048:6144]
    r_rep = poolf[:, 12288:16384]
    xnB = [pool[:, 16384:20480], pool[:, 20480:24576]]
    expst = [pool[:, 16384:17408], pool[:, 17408:18432]]
    kcb = [pool[:, 16384:17408], pool[:, 17408:18432]]
    vcb = [pool[:, 16384:16896], pool[:, 16896:17408]]
    wkbf = pool[:, 20480:24576]
    wvbf = pool[:, 24576:28736]
    xa = [poolf[:, 6144:7168], poolf[:, 7168:8192]]
    tmpGa = poolf[:, 9216:10240]
    tmpGb = poolf[:, 10240:11264]
    kci = [poolf[:, 9216:10240], poolf[:, 14368:15392]]
    vci = [poolf[:, 9216:9728], poolf[:, 9728:10240]]
    wvtl = [wfreg[:, 4096:4104], wfreg[:, 4104:4112]]
    xq_blk = pool[:, 26624:30720]
    lpack = poolf[0:16, 15360:15872]
    lrow = poolf[:, 13312:15360]
    lrec = poolf[0:16, 15872:16384]
    xec = m3f[:, 0:2048]
    tmpE = m3f[:, 2048:4096]
    mu_oh = m4f[:, 0:2048]
    rR2_oh = m4f[:, 2048:4096]
    x_at = wfreg[:, 0:4096]
    cscr = wfreg[:, 4096:5120]
    gp = [wfreg[:, 5120:5632], wfreg[:, 5632:6144]]
    mu_row = wfreg[0:1, 0:4096]
    r_row = wfreg[1:2, 0:4096]
    mo_row = wfreg[2:3, 0:2048]
    rr_row = wfreg[3:4, 0:2048]
    stT = wfreg[0:32, 4096:4480]
    stT2 = wfreg[0:16, 4480:4736]
    wf_ping = [wfreg[:, 0:1024], wfreg[:, 1024:2048]]
    wqf_pp = [wfreg[:, 2048:2560], wfreg[:, 2560:3072]]
    bv_f = wfreg[0:1, 0:520]
    bv_b = wfbf[0:1, 2048:2568]

    sm = {}
    for name, shape, dt in [
        ("st_sums", [128, 32], F32), ("st_sums2", [128, 32], F32),
        ("st_mu", [128, 32], F32), ("st_r", [128, 32], F32),
        ("st_t1", [128, 32], F32), ("st_t2", [128, 32], F32),
        ("st_rv", [128, 32], F32),
        ("s2_mu", [128, 16], F32), ("s2_rr", [128, 16], F32),
        ("s2_t1", [128, 16], F32), ("s2_t2", [128, 16], F32), ("s2_t3", [128, 16], F32),
        ("bq_s", [128, 4], F32), ("bk_s", [128, 4], F32), ("bq_raw", [128, 4], F32),
        ("gln_s", [128, 8], F32), ("bln_s", [128, 8], F32),
        ("eye_s", [128, 128], F32), ("ones1b", [1, 128], BF16),
        ("gacc", [128, 8], F32), ("gmax", [128, 1], F32), ("g11", [1, 1], F32),
        ("b11", [1, 1], F32), ("gcol", [128, 1], F32), ("rgam", [128, 1], F32),
        ("s128g", [128, 1], F32), ("bacc", [128, 8], F32), ("bsum", [128, 1], F32),
        ("beta_c", [128, 1], F32), ("thr_c", [128, 1], F32), ("nthr_c", [128, 1], F32),
        ("gsc_c", [128, 1], F32), ("gsct", [128, 1], F32),
        ("mu_bl", [128, 4], F32), ("r_bl", [128, 4], F32), ("cst", [128, 12], F32),
        ("em4_s", [128, 1], F32), ("ones1f_s", [1, 128], F32),
        ("vsp", [128, 1], F32),
        ("sel16_s", [16, 1024], F32),
    ]:
        sm[name] = nc.alloc_sbuf_tensor(name, shape, dt)
    g = lambda n: sm[n]

    ps = nc.alloc_psum_tensor("ps", [128, 4096], F32)
    ps_sc = [ps[:, 0:1024], ps[:, 1024:2048]]
    ps_av = [ps[0:65, 2048:2560], ps[0:65, 2560:3072]]
    ps_w = [ps[:, 3072:3584], ps[:, 3584:4096]]
    bank = [ps[:, 512 * i:512 * (i + 1)] for i in range(8)]

    sems = {n: nc.alloc_semaphore(n) for n in
            ("qD", "qG", "qK", "qA", "qB", "qX", "qO", "qR", "qS0", "qS1",
             "sV", "sS", "sT")}
    plan = {k: [] for k in ("sync", "gpsimd", "vector", "scalar", "tensor")}
    cnt = {n: 0 for n in sems}

    def wait(eng, sem, val):
        if val and val > 0:
            plan[eng].append(("w", sem, val))

    def op(eng, fn, inc=None, n=1):
        plan[eng].append(("o", fn, inc, n))
        if inc:
            cnt[inc] += n
            return cnt[inc]
        return None

    def dma(eng, q, dst, src):
        return op(eng, lambda e, d=dst, s=src: e.dma_start(d, s), inc=q, n=16)

    T, V, Sc, G, Sy = "tensor", "vector", "scalar", "gpsimd", "sync"

    def replicate_chains(dsts, start_p=1):
        p = start_p
        last = None
        while p < 128:
            wait(G, "qG", cnt["qG"])
            final = 2 * p >= 128
            for dd_ in dsts:
                last = dma(G, "qR" if final else "qG", dd_[p:2 * p, :], dd_[0:p, :])
            p *= 2
        return last

    def vspace():
        op(V, lambda e: e.tensor_scalar(g("vsp")[:, :], g("em4_s")[:, :], 1.0, None,
                                        ALU.mult))

    def _fin():
        if dbg:
            wait(Sy, "sV", cnt["sV"])
            wait(Sy, "sS", cnt["sS"])
            wait(Sy, "sT", cnt["sT"])
            dma(Sy, "qD", dbg_outs["qT"], qT_all[:, :])
            dma(Sy, "qD", dbg_outs["kT"], kT_all[:, :])
            dma(Sy, "qD", dbg_outs["v1"], v1_all[:, :])
            dma(Sy, "qD", dbg_outs["wq"], w_q[:, :])
            dma(Sy, "qD", dbg_outs["pool"], pool[:, :])
            dma(Sy, "qD", dbg_outs["wfreg"], wfreg[:, :])
            dma(Sy, "qD", dbg_outs["stmu"], g("st_mu")[:, :])
            dma(Sy, "qD", dbg_outs["str"], g("st_r")[:, :])
            dma(Sy, "qD", dbg_outs["stsums"], g("st_sums")[:, :])
            dma(Sy, "qD", dbg_outs["stsums2"], g("st_sums2")[:, :])
            dma(Sy, "qD", dbg_outs["beta"], g("beta_c")[:, :])
            dma(Sy, "qD", dbg_outs["thr"], g("thr_c")[:, :])
            dma(Sy, "qD", dbg_outs["nthr"], g("nthr_c")[:, :])
            dma(Sy, "qD", dbg_outs["b11"], g("b11")[:, :])
        return finish(nc, plan, sems, cnt, Sy, wait)

    dma(Sy, "qK", g("bq_raw")[:, :], dd["bq"])
    dma(Sy, "qK", g("bk_s")[:, :], dd["bk"])
    dma(Sy, "qK", g("gln_s")[:, :], dd["gln"])
    dma(Sy, "qK", g("bln_s")[:, :], dd["bln"])
    dma(Sy, "qK", g("eye_s")[:, :], dd["eye32"])
    dma(Sy, "qK", g("ones1b")[:, :], dd["ones1bf"])
    dma(Sy, "qK", g("ones1f_s")[:, :], dd["ones1f"])
    dma(Sy, "qK", g("sel16_s")[:, :], dd["sel16"])
    c_const = dma(Sy, "qK", g("em4_s")[:, :], dd["em4"])
    wait(V, "qK", c_const)
    op(V, lambda e: e.tensor_scalar(g("bq_s")[:, :], g("bq_raw")[:, :], 0.125, None, ALU.mult))

    def stats_chain(n, src_dram, pfx):
        acc = []
        for c in range(n):
            if c >= 2:
                wait(Sy, "sV", acc[c - 2])
            cD = dma(Sy, "qD", xa[c % 2], src_dram[128 * c:128 * (c + 1), :])
            wait(V, "qD", cD)
            op(V, lambda e, i=c % 2: e.tensor_tensor(cscr, xa[i], xa[i], ALU.mult))
            op(V, lambda e, c=c, i=c % 2: e.reduce_sum(g(pfx + "1")[:, c:c + 1], xa[i], AX))
            acc.append(op(V, lambda e, c=c: e.reduce_sum(g(pfx + "2")[:, c:c + 1], x2t, AX),
                          inc="sV"))
        return acc

    for c in range(32):
        if c >= 2:
            wait(Sy, "sV", vlast)
        cD = dma(Sy, "qA" if c % 2 == 0 else "qB", xa[c % 2],
                 dd["x_nat"][128 * c:128 * (c + 1), :])
        wait(V, "qA" if c % 2 == 0 else "qB", cD)
        op(V, lambda e, i=c % 2: e.tensor_tensor(cscr, xa[i], xa[i], ALU.mult))
        op(V, lambda e, c=c, i=c % 2: e.reduce_sum(g("st_sums")[:, c:c + 1], xa[i], AX))
        v_ = op(V, lambda e, c=c: e.reduce_sum(g("st_sums2")[:, c:c + 1], cscr, AX), inc="sV")
        if c >= 1:
            vlast = vprev
        vprev = v_
    vA_last = v_
    op(V, lambda e: e.tensor_scalar(g("st_mu")[:, :], g("st_sums")[:, :], 1.0 / 1024, None, ALU.mult))
    op(V, lambda e: e.tensor_scalar(g("st_t1")[:, :], g("st_sums2")[:, :], 1.0 / 1024, None, ALU.mult))
    vspace()
    op(V, lambda e: e.tensor_tensor(g("st_t2")[:, :], g("st_mu")[:, :], g("st_mu")[:, :], ALU.mult))
    vspace()
    op(V, lambda e: e.tensor_tensor(g("st_t1")[:, :], g("st_t1")[:, :], g("st_t2")[:, :], ALU.subtract))
    vspace()
    op(V, lambda e: e.tensor_scalar(g("st_t2")[:, :], g("st_t1")[:, :], 1e-5, None, ALU.add))
    vspace()
    vA1 = op(V, lambda e: e.reciprocal(g("st_rv")[:, :], g("st_t2")[:, :]), inc="sV")
    wait(Sc, "sV", vA1)
    sA1 = op(Sc, lambda e: e.activation(g("st_r")[:, :], g("st_rv")[:, :], AF.Sqrt), inc="sS")

    for c in range(16):
        wait(Sy, "sV", vprev if c < 2 else vprev2)
        cD = dma(Sy, "qA" if c % 2 == 0 else "qB", xa[c % 2],
                 dd["x_own"][128 * c:128 * (c + 1), :])
        wait(V, "qA" if c % 2 == 0 else "qB", cD)
        op(V, lambda e, i=c % 2: e.tensor_tensor(cscr, xa[i], xa[i], ALU.mult))
        op(V, lambda e, c=c, i=c % 2: e.reduce_sum(g("s2_t1")[:, c:c + 1], xa[i], AX))
        v_ = op(V, lambda e, c=c: e.reduce_sum(g("s2_t2")[:, c:c + 1], cscr, AX), inc="sV")
        if c >= 1:
            vprev2 = vprev
        vprev = v_
    op(V, lambda e: e.tensor_scalar(g("s2_mu")[:, :], g("s2_t1")[:, :], 1.0 / 1024, None, ALU.mult))
    op(V, lambda e: e.tensor_scalar(g("s2_t1")[:, :], g("s2_t2")[:, :], 1.0 / 1024, None, ALU.mult))
    vspace()
    op(V, lambda e: e.tensor_tensor(g("s2_t3")[:, :], g("s2_mu")[:, :], g("s2_mu")[:, :], ALU.mult))
    vspace()
    op(V, lambda e: e.tensor_tensor(g("s2_t1")[:, :], g("s2_t1")[:, :], g("s2_t3")[:, :], ALU.subtract))
    vspace()
    op(V, lambda e: e.tensor_scalar(g("s2_t2")[:, :], g("s2_t1")[:, :], 1e-5, None, ALU.add))
    vspace()
    vB1 = op(V, lambda e: e.reciprocal(g("s2_t3")[:, :], g("s2_t2")[:, :]), inc="sV")
    vspace()
    op(V, lambda e: e.tensor_tensor(g("s2_t1")[:, :], g("s2_t1")[:, :], g("s2_t3")[:, :],
                                    ALU.mult))
    vspace()
    op(V, lambda e: e.tensor_scalar(g("s2_t1")[:, :], g("s2_t1")[:, :], 1e-3, None, ALU.add))
    vspace()
    vB2 = op(V, lambda e: e.reciprocal(g("s2_t2")[:, :], g("s2_t1")[:, :]), inc="sV")
    wait(Sc, "sV", vB2)
    sB1 = op(Sc, lambda e: e.activation(g("st_t2")[:, 0:16], g("s2_t3")[:, :], AF.Sqrt),
             inc="sS")
    sB2 = op(Sc, lambda e: e.activation(g("st_t1")[:, 0:16], g("s2_t2")[:, :], AF.Sqrt),
             inc="sS")
    wait(V, "sS", sB2)
    vspace()
    vB3 = op(V, lambda e: e.tensor_tensor(g("s2_rr")[:, :], g("st_t2")[:, 0:16],
                                          g("st_t1")[:, 0:16], ALU.mult), inc="sV")

    wait(T, "qK", c_const)
    wait(T, "sS", sA1)
    op(T, lambda e: e.matmul(ps[0:32, 2048:2176], g("st_mu")[:, :], g("eye_s")[:, :],
                             start=True, stop=True))
    op(T, lambda e: e.matmul(ps[0:32, 2560:2688], g("st_r")[:, :], g("eye_s")[:, :],
                             start=True, stop=True))
    wait(T, "sV", vB3)
    op(T, lambda e: e.matmul(ps[0:16, 3072:3200], g("s2_mu")[:, :], g("eye_s")[:, :],
                             start=True, stop=True))
    tB2 = op(T, lambda e: e.matmul(ps[0:16, 3584:3712], g("s2_rr")[:, :], g("eye_s")[:, :],
                                   start=True, stop=True), inc="sT")
    wait(V, "sT", tB2)
    op(V, lambda e: e.tensor_copy(stT[0:32, 0:128], ps[0:32, 2048:2176]))
    op(V, lambda e: e.tensor_copy(stT[0:32, 128:256], ps[0:32, 2560:2688]))
    op(V, lambda e: e.tensor_copy(stT2[0:16, 0:128], ps[0:16, 3072:3200]))
    vB4 = op(V, lambda e: e.tensor_copy(stT2[0:16, 128:256], ps[0:16, 3584:3712]), inc="sV")
    wait(G, "sV", vB4)
    dma(G, "qG", mu_row, stT[0:32, 0:128])
    dma(G, "qG", r_row, stT[0:32, 128:256])
    dma(G, "qG", mo_row, stT2[0:16, 0:128])
    dma(G, "qG", rr_row, stT2[0:16, 128:256])
    wait(G, "qG", cnt["qG"])
    dma(G, "qG", mu_rep[0:1, :], mu_row)
    dma(G, "qG", r_rep[0:1, :], r_row)
    dma(G, "qG", mu_oh[0:1, :], mo_row)
    dma(G, "qG", rR2_oh[0:1, :], rr_row)
    replicate_chains((mu_rep, r_rep, mu_oh, rR2_oh))
    cB = cB2 = cnt["qR"]

    if maxphase < 4:
        return _fin()
    vD = []
    for kc in range(NCH):
        if kc >= 2:
            wait(Sy, "sV", vD[kc - 2])
        else:
            wait(Sy, "sV", vB3)
            wait(Sy, "qR", cB)
        cD = dma(Sy, "qA" if kc % 2 == 0 else "qB", wqf_pp[kc % 2],
                 dd["wqT"][128 * kc:128 * (kc + 1), :])
        wait(V, "qA" if kc % 2 == 0 else "qB", cD)
        vD.append(op(V, lambda e, kc=kc: e.tensor_scalar(
            wq_bf[:, JW * kc:JW * (kc + 1)], wqf_pp[kc % 2], 0.125, None, ALU.mult), inc="sV"))
    sD = []
    tDm = []
    vDn = []
    if dsub < 1:
        return _fin()
    for st8 in range(8):
        for kc in range(NCH):
            gi = st8 * NCH + kc
            q_eng, q_sem = (Sy, "qD") if kc % 2 == 0 else (G, "qG")
            if gi >= 2:
                wait(q_eng, "sV", vDn[gi - 2])
            else:
                wait(q_eng, "sV", vD[-1])
            cD = dma(q_eng, q_sem, xa[kc % 2][:, 0:512],
                     dd["xT"][128 * kc:128 * (kc + 1), 512 * st8:512 * (st8 + 1)])
            wait(V, q_sem, cD)
            if gi == 0:
                wait(V, "qR", cB)
                wait(V, "sV", vD[-1])
            if kc == 0 and st8 >= 2 and len(tDm) > st8 - 2:
                wait(V, "sT", tDm[st8 - 2])
            op(V, lambda e, i=kc % 2, st8=st8: e.tensor_tensor(
                xa[i][:, 0:512], xa[i][:, 0:512],
                mu_rep[:, 512 * st8:512 * (st8 + 1)], ALU.subtract))
            vDn.append(op(V, lambda e, i=kc % 2, st8=st8, kc=kc: e.tensor_tensor(
                xnB[st8 % 2][:, 512 * kc:512 * (kc + 1)], xa[i][:, 0:512],
                r_rep[:, 512 * st8:512 * (st8 + 1)], ALU.mult), inc="sV"))
        if dsub < 2:
            continue
        wait(T, "sV", vDn[-1])
        if st8 == 0:
            wait(T, "sV", vB4)
        if st8 >= 2 and dsub >= 3:
            wait(T, "sS", sD[4 * (st8 - 2) + 3])
        for jj in range(4):
            for kc in range(NCH):
                last = (jj == 3 and kc == NCH - 1)
                r = op(T, lambda e, st8=st8, jj=jj, kc=kc: e.matmul(
                    bank[(st8 % 2) * 4 + jj],
                    wq_bf[:, JW * kc + 128 * jj:JW * kc + 128 * (jj + 1)],
                    xnB[st8 % 2][:, 512 * kc:512 * (kc + 1)],
                    start=(kc == 0), stop=(kc == NCH - 1)), inc="sT" if last else None)
        tDm.append(r)
        if dsub < 3:
            continue
        wait(Sc, "sT", tDm[-1])
        for jj in range(4):
            sD.append(op(Sc, lambda e, st8=st8, jj=jj: e.activation(
                qT_all[:, S * jj + 512 * st8:S * jj + 512 * (st8 + 1)],
                bank[(st8 % 2) * 4 + jj], AF.Relu, bias=g("bq_s")[:, jj:jj + 1]), inc="sS"))

    if maxphase < 5:
        return _fin()
    vE = []
    for kc in range(NCH):
        wait(Sy, "sV", vE[kc - 1] if kc >= 1 else vDn[-1])
        cD = dma(Sy, "qD", xec, dd["xTo"][128 * kc:128 * (kc + 1), :])
        wait(V, "qD", cD)
        if kc == 0:
            wait(V, "qR", cB2)
            wait(V, "sS", sD[-1])
        op(V, lambda e: e.tensor_tensor(tmpE, xec, mu_oh, ALU.subtract))
        op(V, lambda e: e.tensor_tensor(tmpE, tmpE, rR2_oh, ALU.mult))
        op(V, lambda e, kc=kc: e.tensor_scalar(xn2(kc), tmpE, g("gln_s")[:, kc:kc + 1],
                                               g("bln_s")[:, kc:kc + 1], ALU.mult, ALU.add))
        vE.append(op(V, lambda e, kc=kc: e.tensor_reduce(
            g("gacc")[:, kc:kc + 1], xn2(kc), AX, ALU.max, apply_absolute_value=True),
            inc="sV"))
    vE1 = op(V, lambda e: e.tensor_reduce(g("gmax")[:, :], g("gacc")[:, :], AX, ALU.max,
                                          apply_absolute_value=True), inc="sV")
    wait(T, "sV", vE1)
    wait(T, "sS", sD[-1])
    tE = op(T, lambda e: e.matmul(ps[0:1, 3072:3200], g("gmax")[:, :], g("eye_s")[:, :],
                                  start=True, stop=True), inc="sT")
    wait(V, "sT", tE)
    vE2 = op(V, lambda e: e.tensor_reduce(g("g11")[:, :], ps[0:1, 3072:3200], AX, ALU.max),
             inc="sV")
    wait(G, "sV", vE2)
    wait(G, "qG", cnt["qG"])
    dma(G, "qG", g("gcol")[0:1, :], g("g11")[:, :])
    replicate_chains((g("gcol"),))
    cE = cnt["qR"]
    wait(V, "qR", cE)
    op(V, lambda e: e.reciprocal(g("rgam")[:, :], g("gcol")[:, :]))
    vspace()
    vE3 = op(V, lambda e: e.tensor_scalar(g("s128g")[:, :], g("rgam")[:, :], QB, None,
                                          ALU.mult), inc="sV")

    if maxphase < 6:
        return _fin()
    vG = []
    for kc in range(NCH):
        wait(Sy, "sV", vG[kc - 2] if kc >= 2 else vD[-1])
        cD = dma(Sy, "qA" if kc % 2 == 0 else "qB", wf_ping[kc % 2],
                 dd["wf"][128 * kc:128 * (kc + 1), :])
        wait(V, "qA" if kc % 2 == 0 else "qB", cD)
        if kc == 0:
            wait(V, "sV", vE3)
        vG.append(op(V, lambda e, kc=kc, i=kc % 2: e.reduce_sum(
            g("bacc")[:, kc:kc + 1], wf_ping[i], AX, apply_absolute_value=True), inc="sV"))
    vspace()
    vG1 = op(V, lambda e: e.reduce_sum(g("bsum")[:, :], g("bacc")[:, :], AX), inc="sV")
    wait(T, "sV", vG1)
    tGb = op(T, lambda e: e.matmul(ps[0:1, 3200:3328], g("bsum")[:, :], g("eye_s")[:, :],
                                   start=True, stop=True), inc="sT")
    wait(V, "sT", tGb)
    vGb = op(V, lambda e: e.reduce_sum(g("b11")[:, :], ps[0:1, 3200:3328], AX), inc="sV")
    wait(G, "sV", vGb)
    wait(G, "qG", cnt["qG"])
    dma(G, "qG", g("beta_c")[0:1, :], g("b11")[:, :])
    replicate_chains((g("beta_c"),))
    cGb = cnt["qR"]
    wait(V, "qR", cGb)
    op(V, lambda e: e.tensor_scalar(g("beta_c")[:, :], g("beta_c")[:, :],
                                    1.0 / (1024.0 * 1024.0), None, ALU.mult))
    vspace()
    op(V, lambda e: e.tensor_scalar(g("thr_c")[:, :], g("beta_c")[:, :], EPS_CLIP, 0.5,
                                    ALU.add, ALU.mult))
    vspace()
    op(V, lambda e: e.tensor_scalar(g("nthr_c")[:, :], g("thr_c")[:, :], -1.0, None, ALU.mult))
    op(V, lambda e: e.tensor_tensor(g("gsct")[:, :], g("gcol")[:, :], g("beta_c")[:, :],
                                    ALU.mult))
    vspace()
    op(V, lambda e: e.tensor_scalar(g("gsc_c")[:, :], g("gsct")[:, :], 1.0 / QB, None, ALU.mult))
    vG2 = []
    for kc in range(NCH):
        wait(Sy, "sV", vG2[kc - 2] if kc >= 2 else vG1)
        cD = dma(Sy, "qA" if kc % 2 == 0 else "qB", wf_ping[kc % 2],
                 dd["wf"][128 * kc:128 * (kc + 1), :])
        wait(V, "qA" if kc % 2 == 0 else "qB", cD)
        op(V, lambda e, kc=kc, i=kc % 2: e.tensor_scalar(tmpGa, wf_ping[i],
                                                         g("thr_c")[:, 0:1], None, ALU.is_gt))
        op(V, lambda e, kc=kc, i=kc % 2: e.tensor_scalar(tmpGb, wf_ping[i],
                                                         g("nthr_c")[:, 0:1], None, ALU.is_lt))
        vG2.append(op(V, lambda e, kc=kc: e.tensor_tensor(w_q[:, D * kc:D * (kc + 1)],
                                                          tmpGa, tmpGb, ALU.subtract),
                      inc="sV"))

    if maxphase < 7:
        return _fin()
    vF = []
    for kc in range(NCH):
        wait(Sy, "sV", vF[kc - 2] if kc >= 2 else vG2[-1])
        cD = dma(Sy, "qA" if kc % 2 == 0 else "qB", wqf_pp[kc % 2],
                 dd["wkT"][128 * kc:128 * (kc + 1), :])
        wait(V, "qA" if kc % 2 == 0 else "qB", cD)
        if kc == 0:
            wait(V, "sV", vG2[-1])
            wait(V, "sT", tDm[-1])
        vF.append(op(V, lambda e, kc=kc, i=kc % 2: e.tensor_copy(
            wkbf[:, JW * kc:JW * (kc + 1)], wqf_pp[i]), inc="sV"))
    vF2 = []
    for kc in range(NCH):
        wait(Sy, "sV", vF2[kc - 2] if kc >= 2 else vF[-1])
        qs = "qA" if kc % 2 == 0 else "qB"
        dma(Sy, qs, wqf_pp[kc % 2], dd["wvT"][128 * kc:128 * (kc + 1), 0:512])
        cD2 = dma(Sy, qs, wvtl[kc % 2], dd["wvT"][128 * kc:128 * (kc + 1), 512:520])
        wait(V, qs, cD2)
        op(V, lambda e, kc=kc, i=kc % 2: e.tensor_copy(wvbf[:, 520 * kc:520 * kc + 512],
                                                       wqf_pp[i]))
        vF2.append(op(V, lambda e, kc=kc, i=kc % 2: e.tensor_copy(
            wvbf[:, 520 * kc + 512:520 * kc + 520], wvtl[i]), inc="sV"))
    wait(Sy, "sV", vF2[-1])
    cbv = dma(Sy, "qD", bv_f, dd["bv"])
    wait(V, "qD", cbv)
    vbv = op(V, lambda e: e.tensor_copy(bv_b, bv_f), inc="sV")

    sF = []
    vK = []
    tK = []
    for tb in range(4):
        for kc in range(NCH):
            gi = tb * NCH + kc
            q_eng, q_sem = (Sy, "qD") if kc % 2 == 0 else (G, "qG")
            if gi >= 2:
                wait(q_eng, "sV", vK[gi - 2])
            else:
                wait(q_eng, "sV", vbv)
            cD = dma(q_eng, q_sem, kci[kc % 2],
                     dd["keysT"][128 * kc:128 * (kc + 1), 1024 * tb:1024 * (tb + 1)])
            wait(V, q_sem, cD)
            if gi == 0:
                wait(V, "sV", vbv)
            if gi >= 2:
                wait(V, "sT", tK[gi - 2])
            vK.append(op(V, lambda e, i=kc % 2: e.tensor_copy(kcb[i], kci[i]), inc="sV"))
            wait(T, "sV", vK[-1])
            if kc == 0:
                if tb == 0:
                    wait(T, "sS", sD[-1])
                else:
                    wait(T, "sS", sF[-1])
            for jj in range(4):
                for th in range(2):
                    last = (jj == 3 and th == 1)
                    cT = op(T, lambda e, jj=jj, th=th, kc=kc, i=kc % 2: e.matmul(
                        bank[jj * 2 + th],
                        wkbf[:, JW * kc + 128 * jj:JW * kc + 128 * (jj + 1)],
                        kcb[i][:, 512 * th:512 * (th + 1)],
                        start=(kc == 0), stop=(kc == NCH - 1)),
                        inc="sT" if last else None)
            tK.append(cT)
        wait(Sc, "sT", cT)
        for jj in range(4):
            for th in range(2):
                sF.append(op(Sc, lambda e, jj=jj, th=th, tb=tb: e.activation(
                    kT_all[:, S * jj + 1024 * tb + 512 * th:S * jj + 1024 * tb + 512 * (th + 1)],
                    bank[jj * 2 + th], AF.Relu, bias=g("bk_s")[:, jj:jj + 1]), inc="sS"))
    vV = []
    vVc = []
    tV = []
    for vb in range(8):
        for kc in range(NCH):
            gi = vb * NCH + kc
            q_eng, q_sem = (Sy, "qD") if kc % 2 == 0 else (G, "qG")
            if gi >= 2:
                wait(q_eng, "sV", vVc[gi - 2])
            else:
                wait(q_eng, "sV", vK[-1])
            cD = dma(q_eng, q_sem, vci[kc % 2],
                     dd["valuesT"][128 * kc:128 * (kc + 1), 512 * vb:512 * (vb + 1)])
            wait(V, q_sem, cD)
            if gi == 0:
                wait(V, "sS", sF[-1])
                wait(V, "sV", vF2[-1])
            if gi >= 2:
                wait(V, "sT", tV[gi - 2])
            vVc.append(op(V, lambda e, i=kc % 2: e.tensor_copy(vcb[i], vci[i]), inc="sV"))
            wait(T, "sV", vVc[-1])
            if kc == 0 and vV:
                wait(T, "sV", vV[-1])
            for tt in range(4):
                for uh in range(2):
                    cTv = op(T, lambda e, tt=tt, uh=uh, kc=kc, i=kc % 2: e.matmul(
                        ps[0:128, 512 * (tt * 2 + uh):512 * (tt * 2 + uh) + 260],
                        vcb[i][:, 128 * tt:128 * (tt + 1)],
                        wvbf[:, 520 * kc + 260 * uh:520 * kc + 260 * (uh + 1)],
                        start=(kc == 0), stop=False),
                        inc="sT" if (tt == 3 and uh == 1) else None)
            tV.append(cTv)
        for tt in range(4):
            for uh in range(2):
                last = (tt == 3 and uh == 1)
                cT = op(T, lambda e, tt=tt, uh=uh: e.matmul(
                    ps[0:128, 512 * (tt * 2 + uh):512 * (tt * 2 + uh) + 260],
                    g("ones1b")[:, :], bv_b[0:1, 260 * uh:260 * (uh + 1)],
                    start=False, stop=True), inc="sT" if last else None)
        wait(V, "sT", cT)
        for tt in range(4):
            for uh in range(2):
                vV.append(op(V, lambda e, vb=vb, tt=tt, uh=uh: e.tensor_scalar(
                    v1_all[:, 520 * (4 * vb + tt) + 260 * uh:
                           520 * (4 * vb + tt) + 260 * (uh + 1)],
                    ps[0:128, 512 * (tt * 2 + uh):512 * (tt * 2 + uh) + 260],
                    0.0, None, ALU.max), inc="sV"))

    if maxphase < 8:
        return _fin()
    exp_seq = []
    av_seq = []
    evac_done = {}
    scram_done = {}
    out_dma_done = 0
    last_combine_v = None
    div_done = None
    wait(T, "sV", vV[-1])
    wait(Sc, "sV", vV[-1])

    def emit_av(hp, st8, tc, gx):
        wait(T, "sS", exp_seq[gx])
        if tc == 0:
            ev = evac_done.get((hp, st8 - 1)) if st8 > 0 else evac_done.get((hp - 1, 7))
            if ev:
                wait(T, "sV", ev)
        eb = expst[gx % 2]
        last = None
        for hh in range(2):
            col = 520 * tc + 65 * (2 * hp + hh)
            last = op(T, lambda e, hh=hh, col=col, eb=eb, tc=tc: e.matmul(
                ps_av[hh], v1_all[:, col:col + 65], eb[:, 512 * hh:512 * (hh + 1)],
                start=(tc == 0), stop=(tc == 31)), inc="sT" if hh == 1 else None)
        av_seq.append(last)

    gidx = 0
    for hp in range(4):
        if hp * 8 >= nst:
            break
        if hp > 0 and do_epi:
            wait(G, "qO", out_dma_done)
        for st8 in range(8):
            if hp * 8 + st8 >= nst:
                break
            for tc in range(32):
                if gidx >= 2:
                    wait(T, "sS", exp_seq[gidx - 2])
                kb = S * hp + 128 * tc
                qb_ = S * hp + 512 * st8
                pp = ps_sc[gidx % 2]
                op(T, lambda e, kb=kb, qb_=qb_, pp=pp: e.matmul(
                    pp[0:64, 0:512], kT_all[0:64, kb:kb + 64],
                    qT_all[0:64, qb_:qb_ + 512], start=True, stop=True, tile_position=(0, 0)))
                op(T, lambda e, kb=kb, qb_=qb_, pp=pp: e.matmul(
                    pp[64:128, 0:512], kT_all[0:64, kb + 64:kb + 128],
                    qT_all[0:64, qb_:qb_ + 512], start=True, stop=True, tile_position=(0, 64)))
                op(T, lambda e, kb=kb, qb_=qb_, pp=pp: e.matmul(
                    pp[0:64, 512:1024], kT_all[64:128, kb:kb + 64],
                    qT_all[64:128, qb_:qb_ + 512], start=True, stop=True,
                    tile_position=(64, 0)))
                csc = op(T, lambda e, kb=kb, qb_=qb_, pp=pp: e.matmul(
                    pp[64:128, 512:1024], kT_all[64:128, kb + 64:kb + 128],
                    qT_all[64:128, qb_:qb_ + 512], start=True, stop=True,
                    tile_position=(64, 64)), inc="sT")
                if tc >= 1:
                    emit_av(hp, st8, tc - 1, gidx - 1)
                wait(Sc, "sT", csc)
                if gidx >= 2:
                    wait(Sc, "sT", av_seq[gidx - 2])
                eb = expst[gidx % 2]
                exp_seq.append(op(Sc, lambda e, pp=pp, eb=eb: e.activation(
                    eb[:, :], pp[:, :], AF.Exp, bias=g("em4_s")[:, 0:1]), inc="sS"))
                gidx += 1
            emit_av(hp, st8, 31, gidx - 1)
            wait(V, "sT", av_seq[-1])
            if hp > 0 and st8 == 0:
                wait(V, "qO", out_dma_done)
            prev = (hp, st8 - 2) if st8 >= 2 else ((hp - 1, 6 + st8) if hp > 0 else None)
            if prev is not None and prev in scram_done:
                wait(V, "qS0" if prev[1] % 2 == 0 else "qS1", scram_done[prev])
            lro = 1024 * (st8 % 2)
            op(V, lambda e, st8=st8: e.tensor_copy(
                aT_pair[0:64, 512 * st8:512 * (st8 + 1)], ps_av[0][0:64, :]))
            op(V, lambda e, st8=st8: e.tensor_copy(
                aT_pair[64:128, 512 * st8:512 * (st8 + 1)], ps_av[1][0:64, :]))
            op(V, lambda e, lro=lro: e.tensor_copy(lrow[64:65, lro:lro + 512],
                                                   ps_av[0][64:65, :]))
            evac_done[(hp, st8)] = op(V, lambda e, lro=lro: e.tensor_copy(
                lrow[64:65, lro + 512:lro + 1024], ps_av[1][64:65, :]), inc="sV")
            wait(G, "sV", evac_done[(hp, st8)])
            qs = "qS0" if st8 % 2 == 0 else "qS1"
            dma(G, qs, lpack[st8:st8 + 1, :], lrow[64:65, lro:lro + 512])
            scram_done[(hp, st8)] = dma(G, qs, lpack[8 + st8:9 + st8, :],
                                        lrow[64:65, lro + 512:lro + 1024])
        if not do_epi or (hp, 7) not in scram_done:
            continue
        wait(V, "qS0", scram_done[(hp, 6)])
        wait(V, "qS1", scram_done[(hp, 7)])
        vrec = op(V, lambda e: e.reciprocal(lrec, lpack), inc="sV")
        wait(Sy, "sV", last_combine_v if last_combine_v else vV[-1])
        dma(Sy, "qX", x_at[0:64, :], dd["x_own"][512 * hp:512 * hp + 256, :])
        cx2 = dma(Sy, "qX", x_at[64:128, :], dd["x_own"][512 * hp + 256:512 * hp + 512, :])
        wait(T, "sV", vrec)
        div_ts = []
        for hh in range(2):
            for k in range(8):
                i = hh * 8 + k
                if i >= 2:
                    wait(T, "sV", div_ts[i - 2])
                cm = op(T, lambda e, hh=hh, k=k, i=i: e.matmul(
                    ps[64 * hh:64 * hh + 64, 2048 + 512 * (i % 2):2560 + 512 * (i % 2)],
                    g("sel16_s")[:, 64 * (8 * hh + k):64 * (8 * hh + k + 1)],
                    lrec[0:16, :],
                    start=True, stop=True), inc="sT")
                wait(V, "sT", cm)
                div_ts.append(op(V, lambda e, hh=hh, k=k, i=i: e.tensor_tensor(
                    aT_pair[64 * hh:64 * hh + 64, 512 * k:512 * (k + 1)],
                    aT_pair[64 * hh:64 * hh + 64, 512 * k:512 * (k + 1)],
                    ps[64 * hh:64 * hh + 64, 2048 + 512 * (i % 2):2560 + 512 * (i % 2)],
                    ALU.mult), inc="sV"))
        wait(V, "qX", cx2)
        for j in range(4):
            op(V, lambda e, j=j: e.tensor_tensor(cscr, x_at[:, 1024 * j:1024 * (j + 1)],
                                                 x_at[:, 1024 * j:1024 * (j + 1)], ALU.mult))
            op(V, lambda e, j=j: e.reduce_sum(g("cst")[:, j:j + 1],
                                              x_at[:, 1024 * j:1024 * (j + 1)], AX))
            op(V, lambda e, j=j: e.reduce_sum(g("cst")[:, 4 + j:5 + j], cscr, AX))
        vspace()
        op(V, lambda e: e.tensor_scalar(g("mu_bl")[:, :], g("cst")[:, 0:4], 1.0 / 1024,
                                        None, ALU.mult))
        op(V, lambda e: e.tensor_scalar(g("cst")[:, 4:8], g("cst")[:, 4:8], 1.0 / 1024,
                                        None, ALU.mult))
        vspace()
        op(V, lambda e: e.tensor_tensor(g("cst")[:, 8:12], g("mu_bl")[:, :],
                                        g("mu_bl")[:, :], ALU.mult))
        vspace()
        op(V, lambda e: e.tensor_tensor(g("cst")[:, 4:8], g("cst")[:, 4:8],
                                        g("cst")[:, 8:12], ALU.subtract))
        vspace()
        op(V, lambda e: e.tensor_scalar(g("cst")[:, 4:8], g("cst")[:, 4:8], 1e-5,
                                        None, ALU.add))
        vspace()
        vv = op(V, lambda e: e.reciprocal(g("cst")[:, 8:12], g("cst")[:, 4:8]), inc="sV")
        wait(Sc, "sV", vv)
        sv = op(Sc, lambda e: e.activation(g("r_bl")[:, :], g("cst")[:, 8:12], AF.Sqrt),
                inc="sS")
        wait(V, "sS", sv)
        for j in range(4):
            op(V, lambda e, j=j: e.tensor_scalar(
                cscr, x_at[:, 1024 * j:1024 * (j + 1)],
                g("mu_bl")[:, j:j + 1], g("r_bl")[:, j:j + 1], ALU.subtract, ALU.mult))
            op(V, lambda e, j=j: e.tensor_tensor(
                aT_pair[:, 1024 * j:1024 * (j + 1)], aT_pair[:, 1024 * j:1024 * (j + 1)],
                cscr, ALU.add))
        for kc in range(NCH):
            op(V, lambda e, kc=kc, hp=hp: e.tensor_scalar(
                cscr[:, 0:512], xn2(kc)[:, 512 * hp:512 * (hp + 1)], g("s128g")[:, 0:1],
                None, ALU.mult))
            op(V, lambda e, kc=kc: e.tensor_scalar(
                xq_blk[:, 512 * kc:512 * (kc + 1)], cscr[:, 0:512],
                -(QB - EPS_CLIP), QB - EPS_CLIP, ALU.max, ALU.min))
        vxq = op(V, lambda e: e.tensor_copy(g("cst")[:, 0:1], g("cst")[:, 0:1]), inc="sV")
        wait(T, "sV", vxq)
        ffs = []
        for tile in range(8):
            j, uu = tile // 2, tile % 2
            pf = ps_w[tile % 2]
            if tile >= 2:
                wait(T, "sS", ffs[tile - 2][0])
            for kc in range(NCH):
                op(T, lambda e, j=j, uu=uu, kc=kc, pf=pf: e.matmul(
                    pf[0:64, 0:512],
                    xq_blk[:, 512 * kc + j:512 * kc + 256:4],
                    w_q[:, D * kc + 512 * uu:D * kc + 512 * (uu + 1)],
                    start=(kc == 0), stop=(kc == NCH - 1), tile_position=(0, 0)))
                cf = op(T, lambda e, j=j, uu=uu, kc=kc, pf=pf: e.matmul(
                    pf[64:128, 0:512],
                    xq_blk[:, 512 * kc + 256 + j:512 * kc + 512:4],
                    w_q[:, D * kc + 512 * uu:D * kc + 512 * (uu + 1)],
                    start=(kc == 0), stop=(kc == NCH - 1), tile_position=(0, 64)),
                    inc="sT" if kc == NCH - 1 else None)
            wait(Sc, "sT", cf)
            if tile >= 2:
                wait(Sc, "sV", ffs[tile - 2][1])
            sg = op(Sc, lambda e, pf=pf, t=tile % 2: e.activation(
                gp[t], pf[:, 0:512], AF.Silu, scale=g("gsc_c")[:, 0:1]), inc="sS")
            wait(V, "sS", sg)
            ffs.append((sg, op(V, lambda e, j=j, uu=uu, t=tile % 2: e.tensor_tensor(
                aT_pair[:, 1024 * j + 512 * uu:1024 * j + 512 * (uu + 1)],
                aT_pair[:, 1024 * j + 512 * uu:1024 * j + 512 * (uu + 1)],
                gp[t], ALU.mult), inc="sV")))
        last_combine_v = op(V, lambda e: e.tensor_tensor(aT_pair[:, :], aT_pair[:, :],
                                                         x_at[:, :], ALU.add), inc="sV")
        wait(Sy, "sV", last_combine_v)
        dma(Sy, "qO", d_out[512 * hp:512 * hp + 256, :], aT_pair[0:64, :])
        out_dma_done = dma(Sy, "qO", d_out[512 * hp + 256:512 * hp + 512, :],
                           aT_pair[64:128, :])
    return _fin()


def finish(nc, plan, sems, cnt, Sy, wait):
    for s in ("qO", "qD", "qG", "qK", "qA", "qB", "qX", "qR", "qS0", "qS1"):
        wait(Sy, s, cnt[s])
    with nc.Block() as blk:
        def runner(items):
            def f(e):
                for it in items:
                    if it[0] == "w":
                        e.wait_ge(sems[it[1]], it[2])
                    else:
                        ins = it[1](e)
                        if it[2]:
                            ins.then_inc(sems[it[2]], it[3])
            return f
        blk.sync(runner(plan["sync"]))
        blk.gpsimd(runner(plan["gpsimd"]))
        blk.vector(runner(plan["vector"]))
        blk.scalar(runner(plan["scalar"]))
        blk.tensor(runner(plan["tensor"]))
    return nc


def make_in_maps(L0_attn_queries, L0_attn_keys, L0_attn_values,
                 L0_attn_Wq_w, L0_attn_Wq_b, L0_attn_Wk_w, L0_attn_Wk_b,
                 L0_attn_Wv_w, L0_attn_Wv_b,
                 L0_ffn_weight, L0_ffn_ln_gamma, L0_ffn_ln_beta):
    x = np.ascontiguousarray(np.asarray(L0_attn_queries, np.float32))
    keysT = np.ascontiguousarray(np.asarray(L0_attn_keys, np.float32)[0].T)
    valuesT = np.ascontiguousarray(np.asarray(L0_attn_values, np.float32)[0].T)
    wq = np.asarray(L0_attn_Wq_w, np.float32)
    wk = np.asarray(L0_attn_Wk_w, np.float32)
    wv = np.asarray(L0_attn_Wv_w, np.float32)
    bqv = np.asarray(L0_attn_Wq_b, np.float32)
    bkv = np.asarray(L0_attn_Wk_b, np.float32)
    bvv = np.asarray(L0_attn_Wv_b, np.float32)
    wf = np.ascontiguousarray(np.asarray(L0_ffn_weight, np.float32))
    glnv = np.asarray(L0_ffn_ln_gamma, np.float32)
    blnv = np.asarray(L0_ffn_ln_beta, np.float32)
    eye = np.eye(128, dtype=np.float32)
    ones1bf = np.ones((1, 128), bf16)
    sel16 = np.zeros((16, 1024), np.float32)
    for i in range(16):
        sel16[i, 64 * i:64 * (i + 1)] = 1.0

    in_maps = []
    for c in range(8):
        b, q = c // 2, c % 2
        R0 = 2048 * q
        J = slice(512 * q, 512 * (q + 1))
        xb = x[b]
        xbT = np.ascontiguousarray(xb.T)
        wvT = np.zeros((1024, 520), np.float32)
        bv = np.zeros((1, 520), np.float32)
        for h in range(8):
            wvT[:, 65 * h:65 * h + 64] = wv[512 * q + 64 * h:512 * q + 64 * (h + 1), :].T
            bv[0, 65 * h:65 * h + 64] = bvv[512 * q + 64 * h:512 * q + 64 * (h + 1)]
            bv[0, 65 * h + 64] = 1.0
        in_maps.append({
            "x_own": np.ascontiguousarray(xb[R0:R0 + 2048]),
            "x_nat": xb,
            "xT": xbT,
            "xTo": np.ascontiguousarray(xbT[:, R0:R0 + 2048]),
            "keysT": keysT,
            "valuesT": valuesT,
            "wqT": np.ascontiguousarray(wq[J].T),
            "wkT": np.ascontiguousarray(wk[J].T),
            "wvT": wvT,
            "bq": np.ascontiguousarray(bqv[J].reshape(4, 128).T),
            "bk": np.ascontiguousarray(bkv[J].reshape(4, 128).T),
            "bv": bv,
            "wf": wf,
            "gln": np.ascontiguousarray(glnv.reshape(8, 128).T),
            "bln": np.ascontiguousarray(blnv.reshape(8, 128).T),
            "eye32": eye,
            "ones1bf": ones1bf,
            "ones1f": np.ones((1, 128), np.float32),
            "sel16": sel16,
            "em4": np.full((128, 1), EXP_SHIFT, np.float32),
        })
    return in_maps


def kernel(**inputs):
    in_maps = make_in_maps(**inputs)
    nc = build_nc()
    res = run_bass_kernel_spmd(nc, in_maps, core_ids=list(range(8)))
    out = np.empty((4, 4096, 1024), np.float32)
    for c in range(8):
        b, q = c // 2, c % 2
        out[b, 2048 * q:2048 * (q + 1), :] = res.results[c]["out"]
    return out
